# version 02
# baseline (speedup 1.0000x reference)
import numpy as np
import concourse.bass as bass
import concourse.mybir as mybir
from concourse.bass_utils import run_bass_kernel_spmd

F32 = mybir.dt.float32
AF = mybir.ActivationFunctionType
ALU = mybir.AluOpType
AX = mybir.AxisListType.X

P = 128
C = 1024
CT = 8
T = 16384
NCOR = 8
S = T // NCOR
TC = 512
NCH = T // TC
OB = 512
NB = S // OB
FF = 4096
FT = FF // P
NE = 8
FCH = 256
NFC = FF // FCH
EPS = 1e-5
NEG = -1e38
DBG = {}


ENG_ALIAS = {"sy": "sync", "gp": "gpsimd", "te": "tensor", "v": "vector", "s": "scalar"}


class Sch:
    def __init__(self, nc):
        self.nc = nc
        self.ops = {k: [] for k in ("sync", "gpsimd", "tensor", "vector", "scalar")}
        self.sems = {}
        self.cnt = {}

    def sem(self, name):
        if name not in self.sems:
            self.sems[name] = self.nc.semaphore(name).__enter__()
            self.cnt[name] = 0
        return self.sems[name]

    drain_inc = True

    def op(self, eng, fn, inc=None, inc_by=1):
        eng = ENG_ALIAS.get(eng, eng)
        if inc is not None:
            s = self.sem(inc)
            self.cnt[inc] += inc_by
            if self.drain_inc and eng in ("vector", "scalar"):
                self.ops[eng].append(lambda e, fn=fn: fn(e))
                self.ops[eng].append(lambda e, s=s, n=inc_by: e.drain().then_inc(s, n))
            else:
                self.ops[eng].append(lambda e, fn=fn, s=s, n=inc_by: fn(e).then_inc(s, n))
        else:
            self.ops[eng].append(lambda e, fn=fn: fn(e))

    def wait(self, eng, sem, val=None):
        s = self.sem(sem)
        v = self.cnt[sem] if val is None else val
        self.ops[eng].append(lambda e, s=s, v=v: e.wait_ge(s, v))

    def barrier(self, engines=("sync", "gpsimd", "tensor", "vector", "scalar")):
        if "sync" in engines:
            for q in ("qL", "qO"):
                if q in self.cnt:
                    self.wait("sync", q)
        if "gpsimd" in engines and "qG" in self.cnt:
            self.wait("gpsimd", "qG")
        for eng in engines:
            self.op(eng, lambda e: e.nop(), inc="bar")
        v = self.cnt["bar"]
        for eng in engines:
            self.wait(eng, "bar", v)


def build():
    nc = bass.Bass("TRN2", target_bir_lowering=False)
    sch = Sch(nc)

    def din(name, shape):
        return nc.dram_tensor(name, list(shape), F32, kind="ExternalInput").ap()

    x_full = din("x", [T, C])
    xob = din("xob", [2176, C])
    Wk = din("Wk", [P, CT, C]); Wv = din("Wv", [P, CT, C]); Wr = din("Wr", [P, CT, C])
    Wck = din("Wck", [P, CT, FF]); Wcr = din("Wcr", [P, CT, C])
    Wcv = din("Wcv", [P, FT, C])
    Wg = din("Wg", [NE, P, CT, FF]); Wu = din("Wu", [NE, P, CT, FF])
    Wd = din("Wd", [NE, P, FT, C])
    Wrg = din("Wrg", [P, CT, 16])
    wrow8 = din("wrow8", [1, CT * P])
    ucol = din("ucol", [P, CT])
    g2col = din("g2col", [P, CT]); b2col = din("b2col", [P, CT])
    tshcol = din("tshcol", [P, CT])
    g1row = din("g1row", [1, C]); b1row = din("b1row", [1, C])
    niota = din("niota", [1, T])
    identin = din("identin", [P, P])
    ones1 = din("ones1", [1, P])
    onescol = din("onescol", [P, 1])
    selv = din("selv", [P, NCH])
    selb = din("selb", [P, NCH])
    bselin = din("bsel", [P, 1]); nbselin = din("nbsel", [P, 1])

    out_own = nc.dram_tensor("out_own", [S, C], F32, kind="ExternalOutput").ap()

    yscr = nc.dram_tensor("yscr", [C, S], F32, kind="Internal").ap()
    ybscr = nc.dram_tensor("ybscr", [C, 1], F32, kind="Internal").ap()
    xnscr = nc.dram_tensor("xnscr", [C, 2176], F32, kind="Internal").ap()
    hscr = nc.dram_tensor("hscr", [C, 2176], F32, kind="Internal").ap()
    sigrscr = nc.dram_tensor("sigrscr", [C, S + 1], F32, kind="Internal").ap()
    cwscr = nc.dram_tensor("cwscr", [NE, S], F32, kind="Internal").ap()
    flushscr = nc.dram_tensor("flushscr", [P, CT * (TC + 1)], F32, kind="Internal").ap()

    dbg_outs = {}

    def fbar(ap2):
        sch.wait("sync", "sv")
        sch.wait("sync", "ss")
        sch.wait("sync", "st")
        dma("sy", flushscr, ap2, q="qO")
        sch.wait("vector", "qO")
        sch.wait("scalar", "qO")
        sch.wait("tensor", "qO")

    def gptap(name, ap):
        if not DBG.get(name):
            return
        t = nc.dram_tensor("dbg_" + name, [ap.shape[0], int(np.prod(ap.shape[1:]))],
                           F32, kind="ExternalOutput").ap()
        dbg_outs[name] = t
        sch.wait("gpsimd", "sv")
        flat = ap if len(ap.shape) == 2 else ap.rearrange("p t n -> p (t n)")
        dma("gp", t, flat, q="qG")

    def tap(name, ap):
        if not DBG.get(name):
            return
        t = nc.dram_tensor("dbg_" + name, [ap.shape[0], int(np.prod(ap.shape[1:]))],
                           F32, kind="ExternalOutput").ap()
        dbg_outs[name] = t
        sch.wait("sync", "sv")
        sch.wait("sync", "ss")
        sch.wait("sync", "st")
        flat = ap if len(ap.shape) == 2 else ap.rearrange("p t n -> p (t n)")
        dma("sy", t, flat, q="qO")
        sch.wait("vector", "qO")
        sch.wait("scalar", "qO")
        sch.wait("tensor", "qO")

    A = nc.alloc_sbuf_tensor
    r3 = lambda ap2, t: ap2.rearrange("p (t n) -> p t n", t=t)
    kapF = A("kapA", [P, CT * TC], F32).ap();      kap = r3(kapF, CT)
    varF = A("varA", [P, CT * TC], F32).ap();      var = r3(varF, CT)
    ppF = A("ppA", [P, CT * (TC + 1)], F32).ap();  pp = r3(ppF, CT)
    lgF = A("lgA", [P, CT * (TC + 1)], F32).ap();  lg = r3(lgF, CT)
    s0F = A("s0A", [P, CT * (TC + 1)], F32).ap();  s0 = r3(s0F, CT)
    s1F = A("s1A", [P, CT * (TC + 1)], F32).ap();  s1 = r3(s1F, CT)
    s2F = A("s2A", [P, CT * (TC + 1)], F32).ap();  s2 = r3(s2F, CT)
    hfm = s0
    xt = A("xt", [P, C], F32).ap()
    xc_t = A("xc_t", [P, C], F32).ap()
    xn_t = A("xn_t", [P, C], F32).ap()
    h_t = A("h_t", [P, C], F32).ap()
    tmp_t = A("tmp_t", [P, C], F32).ap()
    wbuf = A("wbuf", [P, CT * P], F32).ap()
    wbuf2 = A("wbuf2", [P, CT * P], F32).ap()
    wfbuf = A("wfbuf", [P, CT * FCH], F32).ap()
    wfbuf2 = A("wfbuf2", [P, CT * FCH], F32).ap()
    ident = A("identS", [P, P], F32).ap()
    sb_ones1 = A("ones1S", [1, P], F32).ap()
    sb_onescol = A("onescolS", [P, 1], F32).ap()
    sb_wrow = A("wrowS", [1, CT * P], F32).ap()
    sb_u = A("uS", [P, CT], F32).ap()
    sb_g2 = A("g2S", [P, CT], F32).ap()
    sb_b2 = A("b2S", [P, CT], F32).ap()
    sb_tsh = A("tshS", [P, CT], F32).ap()
    sb_selv = A("selvS", [P, NCH], F32).ap()
    sb_selb = A("selbS", [P, NCH], F32).ap()
    bsel = A("bselS", [P, 1], F32).ap()
    nbsel = A("nbselS", [P, 1], F32).ap()
    g1b = A("g1bS", [P, C], F32).ap()
    b1b = A("b1bS", [P, C], F32).ap()
    sb_niota = A("niotaS", [1, TC], F32).ap()
    mb = A("mbS", [P, OB + 1], F32).ap()
    rb = A("rbS", [P, OB + 1], F32).ap()
    statrow = A("statrowS", [1, 2 * (OB + 1)], F32).ap()
    lgbuf = A("lgbufS", [16, 512], F32).ap()
    tmb = A("tmbS", [P, 64], F32).ap()
    cwland = A("cwlandS", [1, NE * OB], F32).ap()
    temp5 = A("temp5S", [P, OB + 1], F32).ap()
    tmp6 = A("tmp6S", [P, OB + 1], F32).ap()
    stat = A("statS", [P, 8], F32).ap()
    d12 = A("d12S", [P, 2 * CT], F32).ap()
    e12 = A("e12S", [P, 2 * CT], F32).ap()
    bbA = A("bbAS", [P, CT], F32).ap()
    bbB = A("bbBS", [P, CT], F32).ap()
    t1c = A("t1cS", [P, CT], F32).ap()
    qc = A("qcS", [P, CT], F32).ap()
    dcol = A("dcolS", [P, CT], F32).ap()
    carryA = A("carryAS", [P, CT], F32).ap()
    hcar = A("hcarS", [P, CT], F32).ap()
    ppcar = A("ppcarS", [P, CT], F32).ap()
    lgcar = A("lgcarS", [P, CT], F32).ap()

    PS = nc.alloc_psum_tensor
    psK = PS("psK", [P, TC], F32).ap()
    psG = PS("psG", [P, OB], F32).ap()
    psU = PS("psU", [P, OB], F32).ap()
    psD = PS("psD", [P, OB], F32).ap()
    psT = PS("psT", [P, P], F32).ap()
    psB = PS("psB", [P, TC], F32).ap()
    psR = PS("psR", [1, TC], F32).ap()

    bbs = [bbA, bbB]

    def dma(eng, out, in_, q="qL", accum=None, ncd=False):
        kw = {}
        if accum is not None:
            kw["accum_op"] = accum
        if ncd:
            kw["allow_slow_non_contiguous"] = True
        sch.op(eng, lambda e, o=out, i=in_, kw=kw: e.dma_start(out=o, in_=i, **kw),
               inc=q, inc_by=16)

    def fl(a3, n, o=0):
        ap = a3[:, :, o:o + n]
        try:
            return ap.rearrange("p t n -> p (t n)")
        except ValueError:
            return ap

    def act(out, in_, func, bias=None, scale=None, inc=None):
        kw = {}
        if bias is not None:
            kw["bias"] = bias
        if scale is not None:
            kw["scale"] = scale
        sch.op("s", lambda e, o=out, i=in_, f=func, kw=kw: e.activation(o, i, f, **kw), inc=inc)

    def vop(fn, inc=None):
        sch.op("v", fn, inc=inc)

    def emit_linear(evict_cb, wt_dram, rhs3, ncols, mtiles, ktiles=CT, extra=None):
        bufs = [wbuf, wbuf2]
        for mt in range(mtiles):
            b = bufs[mt % 2]
            sch.wait("sync", "sv")
            sch.wait("sync", "st")
            dma("sy", b[:, 0:ktiles * P].rearrange("p (t n) -> p t n", t=ktiles),
                wt_dram[:, :, mt * P:(mt + 1) * P])
            sch.wait("tensor", "qL")
            sch.wait("tensor", "sv")
            for kt in range(ktiles):
                last = (kt == ktiles - 1) and (extra is None)
                sch.op("te", lambda e, b=b, kt=kt, last=last, rhs3=rhs3, nn=ncols:
                       e.matmul(psK[:, 0:nn], b[:, kt * P:(kt + 1) * P],
                                rhs3[:, kt, 0:nn], start=(kt == 0), stop=last),
                       inc=("st" if last else None))
            if extra is not None:
                lh, rh = extra(mt)
                sch.op("te", lambda e, lh=lh, rh=rh, nn=ncols:
                       e.matmul(psK[:, 0:nn], lh, rh, start=False, stop=True), inc="st")
            sch.wait("vector", "st")
            sch.wait("scalar", "st")
            evict_cb(mt, psK)

    def emit_ln_tile(src_rows, want_h):
        sch.wait("sync", "sv")
        dma("sy", xt, src_rows)
        sch.wait("vector", "qL")
        vop(lambda e: e.tensor_reduce(stat[:, 0:1], xt, AX, ALU.add))
        vop(lambda e: e.drain())
        vop(lambda e: e.tensor_scalar(stat[:, 1:2], stat[:, 0:1], 1.0 / C, None, op0=ALU.mult))
        vop(lambda e: e.drain())
        vop(lambda e: e.tensor_scalar(xc_t, xt, stat[:, 1:2], None, op0=ALU.subtract), inc="sv")
        sch.wait("scalar", "sv")
        act(tmp_t, xc_t, AF.Square, inc="ss")
        sch.wait("vector", "ss")
        vop(lambda e: e.tensor_reduce(stat[:, 2:3], tmp_t, AX, ALU.add))
        vop(lambda e: e.drain())
        vop(lambda e: e.tensor_scalar(stat[:, 3:4], stat[:, 2:3], 1.0 / C, None, op0=ALU.mult))
        vop(lambda e: e.drain())
        vop(lambda e: e.tensor_scalar(stat[:, 6:7], stat[:, 3:4], EPS, None, op0=ALU.add))
        vop(lambda e: e.drain(), inc="sv")
        sch.wait("scalar", "sv")
        act(stat[:, 4:5], stat[:, 6:7], AF.Ln)
        act(stat[:, 5:6], stat[:, 4:5], AF.Exp, scale=-0.5, inc="ss")
        sch.wait("vector", "ss")
        vop(lambda e: e.tensor_scalar(xn_t, xc_t, stat[:, 5:6], None, op0=ALU.mult), inc="sv")
        if not want_h:
            return
        vop(lambda e: e.tensor_reduce(stat[:, 0:1], xn_t, AX, ALU.add))
        vop(lambda e: e.drain())
        vop(lambda e: e.tensor_scalar(stat[:, 1:2], stat[:, 0:1], 1.0 / C, None, op0=ALU.mult))
        vop(lambda e: e.drain())
        vop(lambda e: e.tensor_scalar(xc_t, xn_t, stat[:, 1:2], None, op0=ALU.subtract), inc="sv")
        sch.wait("scalar", "sv")
        act(tmp_t, xc_t, AF.Square, inc="ss")
        sch.wait("vector", "ss")
        vop(lambda e: e.tensor_reduce(stat[:, 2:3], tmp_t, AX, ALU.add))
        vop(lambda e: e.drain())
        vop(lambda e: e.tensor_scalar(stat[:, 3:4], stat[:, 2:3], 1.0 / C, None, op0=ALU.mult))
        vop(lambda e: e.drain())
        vop(lambda e: e.tensor_scalar(stat[:, 6:7], stat[:, 3:4], EPS, None, op0=ALU.add))
        vop(lambda e: e.drain(), inc="sv")
        sch.wait("scalar", "sv")
        act(stat[:, 4:5], stat[:, 6:7], AF.Ln)
        act(stat[:, 5:6], stat[:, 4:5], AF.Exp, scale=-0.5, inc="ss")
        sch.wait("vector", "ss")
        vop(lambda e: e.tensor_scalar(tmp_t, xc_t, stat[:, 5:6], None, op0=ALU.mult))
        vop(lambda e: e.tensor_tensor(h_t, tmp_t, g1b, ALU.mult))
        vop(lambda e: e.tensor_tensor(h_t, h_t, b1b, ALU.add), inc="sv")

    def load_fm(dram, c0, n, dst3):
        for ct in range(CT):
            dma("sy", dst3[:, ct, 0:n], dram[ct * P:(ct + 1) * P, c0:c0 + n], ncd=(n < 16))

    for sb, dr in [(ident, identin), (sb_ones1, ones1), (sb_onescol, onescol),
                   (sb_wrow, wrow8), (sb_u, ucol), (sb_g2, g2col), (sb_b2, b2col),
                   (sb_tsh, tshcol), (sb_selv, selv), (sb_selb, selb),
                   (bsel, bselin), (nbsel, nbselin)]:
        dma("sy", sb, dr)
    for eng in ("tensor", "vector", "scalar", "gpsimd"):
        sch.wait(eng, "qL")
    for srcd, dst in ((g1row, g1b), (b1row, b1b)):
        sch.wait("sync", "sv")
        dma("sy", statrow[0:1, 0:C], srcd)
        sch.wait("tensor", "qL")
        for half in range(2):
            c0 = half * 512
            sch.op("te", lambda e, c0=c0: e.matmul(
                psB[:, 0:512], sb_ones1, statrow[0:1, c0:c0 + 512], start=True, stop=True), inc="st")
            sch.wait("vector", "st")
            vop(lambda e, dst=dst, c0=c0: e.tensor_copy(dst[:, c0:c0 + 512], psB[:, 0:512]), inc="sv")
            sch.wait("tensor", "sv")
    sch.barrier()

    for tile in range(2176 // P):
        r0 = tile * P
        emit_ln_tile(xob[r0:r0 + P, :], want_h=True)
        for src_t, dscr in ((xn_t, xnscr), (h_t, hscr)):
            for ct in range(CT):
                sch.wait("tensor", "sv")
                sch.op("te", lambda e, src_t=src_t, ct=ct: e.transpose(
                    psT, src_t[:, ct * P:(ct + 1) * P], ident), inc="st")
                sch.wait("vector", "st")
                sch.wait("vector", "qO")
                vop(lambda e: e.tensor_copy(temp5[:, 0:P], psT), inc="sv")
                sch.wait("sync", "sv")
                dma("sy", dscr[ct * P:(ct + 1) * P, r0:r0 + P], temp5[:, 0:P], q="qO")
    sch.wait("vector", "qO")
    sch.barrier()

    for oc in range(5):
        c0 = oc * 512
        n = 512 if oc < 4 else 1
        load_fm(hscr, c0, n, s1)
        sch.wait("vector", "qL")
        if oc == 0:
            vop(lambda e: e.tensor_scalar(t1c, s1[:, :, 1], bsel[:, 0:1], None, op0=ALU.mult))
            vop(lambda e: e.tensor_scalar(qc, sb_tsh, nbsel[:, 0:1], None, op0=ALU.mult))
            vop(lambda e: e.drain())
            vop(lambda e: e.tensor_tensor(s1[:, :, 1], t1c, qc, ALU.add), inc="sv")
            sch.wait("tensor", "sv")

        def evict_r(mt, ps, n=n):
            act(s0[:, mt, 0:n], ps[:, 0:n], AF.Exp, scale=-1.0, inc="ss")
            sch.wait("vector", "ss")
            vop(lambda e, mt=mt, n=n: e.tensor_scalar(s0[:, mt, 0:n], s0[:, mt, 0:n],
                                                      1.0, None, op0=ALU.add), inc="sv")
            sch.wait("scalar", "sv")
            act(s2[:, mt, 0:n], s0[:, mt, 0:n], AF.Ln, inc="ss")
            sch.wait("scalar", "ss")
            act(s0[:, mt, 0:n], s2[:, mt, 0:n], AF.Exp, scale=-1.0, inc="ss")
            sch.wait("vector", "ss")
            vop(lambda e, mt=mt, n=n: e.tensor_copy(s2[:, mt, 0:n], s0[:, mt, 0:n]), inc="sv")

        emit_linear(evict_r, Wr, s1, n, CT)
        sch.wait("sync", "sv")
        for ct in range(CT):
            dma("sy", sigrscr[ct * P:(ct + 1) * P, c0:c0 + n], s2[:, ct, 0:n], q="qO", ncd=(n < 16))
        sch.wait("vector", "qO")
        sch.barrier()

    dma("sy", wbuf[:, 0:CT * 16].rearrange("p (t n) -> p t n", t=CT), Wrg)
    for oc in range(4):
        load_fm(xnscr, 2 + oc * 512, 512, s0)
        sch.wait("tensor", "qL")
        for kt in range(CT):
            sch.op("te", lambda e, kt=kt: e.matmul(psB[0:16, 0:512], wbuf[:, kt * 16:(kt + 1) * 16],
                                                   s0[:, kt, 0:512], start=(kt == 0), stop=(kt == CT - 1)),
                   inc=("st" if kt == CT - 1 else None))
        sch.wait("vector", "st")
        vop(lambda e: e.tensor_copy(lgbuf, psB[0:16, 0:512]), inc="sv")
        sch.wait("tensor", "sv")
        for q4 in range(4):
            sch.wait("tensor", "sv")
            sch.op("te", lambda e, q4=q4: e.transpose(psT[:, 0:16], lgbuf[:, q4 * P:(q4 + 1) * P], ident[0:16, 0:16]), inc="st")
            sch.wait("vector", "st")
            vop(lambda e: e.tensor_copy(tmb[:, 0:16], psT[:, 0:16]), inc="sv")
            vop(lambda e: e.drain())
            vop(lambda e: e.tensor_scalar(tmb[:, 8:16], tmb[:, 8:16], 10.0, None, op0=ALU.mult))
            vop(lambda e: e.drain())
            for hh in range(2):
                cs = slice(hh * 8, hh * 8 + 8)
                sh = slice(24 + hh * 8, 32 + hh * 8)
                ex_ = slice(40 + hh * 8, 48 + hh * 8)
                vop(lambda e, cs=cs: e.tensor_reduce(stat[:, 0:1], tmb[:, cs], AX, ALU.max))
                vop(lambda e: e.drain())
                vop(lambda e, cs=cs, sh=sh: e.tensor_scalar(tmb[:, sh], tmb[:, cs], stat[:, 0:1], None,
                                                            op0=ALU.subtract))
                vop(lambda e: e.drain(), inc="sv")
                sch.wait("scalar", "sv")
                sch.op("s", lambda e, sh=sh, ex_=ex_: e.activation(tmb[:, ex_], tmb[:, sh], AF.Exp), inc="ss")
                sch.wait("vector", "ss")
                vop(lambda e, ex_=ex_: e.tensor_reduce(stat[:, 1:2], tmb[:, ex_], AX, ALU.add))
                vop(lambda e: e.drain())
                vop(lambda e: e.reciprocal(stat[:, 2:3], stat[:, 1:2]))
                vop(lambda e: e.drain())
                vop(lambda e, cs=cs, ex_=ex_: e.tensor_scalar(tmb[:, cs], tmb[:, ex_], stat[:, 2:3], None,
                                                              op0=ALU.mult))
                vop(lambda e: e.drain())
            vop(lambda e: e.tensor_tensor(tmb[:, 16:24], tmb[:, 0:8], tmb[:, 8:16], ALU.mult))
            vop(lambda e: e.drain())
            vop(lambda e: e.tensor_reduce(stat[:, 3:4], tmb[:, 16:24], AX, ALU.add))
            vop(lambda e: e.drain())
            vop(lambda e: e.tensor_scalar(stat[:, 3:4], stat[:, 3:4], 1e-9, None, op0=ALU.add))
            vop(lambda e: e.drain())
            vop(lambda e: e.reciprocal(stat[:, 4:5], stat[:, 3:4]))
            vop(lambda e: e.drain())
            vop(lambda e: e.tensor_scalar(tmb[:, 16:24], tmb[:, 16:24], stat[:, 4:5], None, op0=ALU.mult), inc="sv")
            sch.wait("tensor", "sv")
            sch.op("te", lambda e: e.transpose(psT[0:8, 0:P], tmb[:, 16:24], ident), inc="st")
            sch.wait("vector", "st")
            sch.wait("vector", "qO")
            vop(lambda e: e.tensor_copy(temp5[0:8, 0:P], psT[0:8, 0:P]), inc="sv")
            sch.wait("sync", "sv")
            dma("sy", cwscr[0:8, oc * 512 + q4 * P: oc * 512 + (q4 + 1) * P], temp5[0:8, 0:P], q="qO")
    sch.wait("vector", "qO")
    sch.barrier()

    vop(lambda e: e.memset(ppcar, NEG))
    vop(lambda e: e.memset(lgcar, 0.0))
    vop(lambda e: e.memset(bbA, 1.0))
    vop(lambda e: e.memset(carryA, 0.0))
    vop(lambda e: e.tensor_copy(hcar, sb_tsh), inc="sv")
    sch.barrier()

    for c in range(NCH):
        t0 = c * TC
        vop(lambda e: e.tensor_copy(hfm[:, :, 0], hcar))
        vop(lambda e: e.tensor_copy(pp[:, :, 0], ppcar))
        vop(lambda e: e.tensor_copy(lg[:, :, 0], lgcar))
        vop(lambda e: e.drain(), inc="sv")
        sch.wait("tensor", "sv")
        for q4 in range(4):
            r0 = t0 + q4 * P
            emit_ln_tile(x_full[r0:r0 + P, :], want_h=True)
            for ct in range(CT):
                sch.wait("tensor", "sv")
                sch.op("te", lambda e, ct=ct: e.transpose(psT, h_t[:, ct * P:(ct + 1) * P], ident), inc="st")
                sch.wait("vector", "st")
                vop(lambda e, ct=ct, q4=q4: e.tensor_copy(hfm[:, ct, 1 + q4 * P:1 + (q4 + 1) * P], psT), inc="sv")
        vop(lambda e: e.drain())
        vop(lambda e: e.tensor_copy(hcar, hfm[:, :, TC]), inc="sv")
        sch.wait("sync", "sv")
        dma("sy", sb_niota, niota[:, t0:t0 + TC])
        sch.wait("tensor", "qL")

        def evict_k(mt, ps):
            vop(lambda e, mt=mt: e.tensor_copy(kap[:, mt, :], ps[:, 0:TC]), inc="sv")

        def evict_v(mt, ps):
            vop(lambda e, mt=mt: e.tensor_copy(var[:, mt, :], ps[:, 0:TC]), inc="sv")

        emit_linear(evict_k, Wk, hfm, TC, CT,
                    extra=lambda mt: (sb_wrow[0:1, mt * P:(mt + 1) * P], sb_niota))
        emit_linear(evict_v, Wv, hfm, TC, CT)
        if c == 0:
            tap("hfm0", s0F)
            tap("kap0", kapF)
            tap("var0", varF)
        if c == 1:
            tap("kap1", kapF)
            tap("var1", varF)
        sch.barrier(("vector", "tensor", "scalar", "sync"))

        sch.drain_inc = False
        vop(lambda e: e.tensor_tensor(dcol, pp[:, :, 0], kap[:, :, 0], ALU.subtract))
        vop(lambda e: e.tensor_tensor(qc, pp[:, :, 0], kap[:, :, 0], ALU.max))
        vop(lambda e: e.tensor_scalar(d12[:, 0:CT], dcol, 0.0, None, op0=ALU.min))
        vop(lambda e: e.tensor_scalar(d12[:, CT:2 * CT], dcol, 0.0, -1.0,
                                      op0=ALU.max, op1=ALU.mult), inc="sv")
        sch.wait("scalar", "sv")
        for i in range(TC):
            bbp, bbn_ = bbs[i % 2], bbs[(i + 1) % 2]
            sch.op("s", lambda e: e.activation(e12, d12, AF.Exp), inc="ss")
            sch.wait("vector", "ss")
            vop(lambda e, bbp=bbp: e.tensor_tensor(t1c, e12[:, 0:CT], bbp, ALU.mult))
            if i < TC - 1:
                vop(lambda e, i=i: e.tensor_tensor(dcol, qc, kap[:, :, i + 1], ALU.subtract))
            else:
                vop(lambda e: e.nop())
            vop(lambda e, bbn_=bbn_: e.tensor_tensor(bbn_, t1c, e12[:, CT:2 * CT], ALU.add), inc="sv")
            sch.wait("scalar", "sv")
            sch.op("s", lambda e, bbn_=bbn_, i=i: e.activation(lg[:, :, i + 1], bbn_, AF.Ln), inc="ss")
            sch.wait("vector", "ss")
            vop(lambda e, i=i: e.tensor_tensor(pp[:, :, i + 1], qc, lg[:, :, i + 1], ALU.add))
            if i < TC - 1:
                vop(lambda e, i=i: e.tensor_tensor(dcol, dcol, lg[:, :, i + 1], ALU.add))
                vop(lambda e, i=i: e.tensor_tensor(qc, pp[:, :, i + 1], kap[:, :, i + 1], ALU.max))
                vop(lambda e: e.tensor_scalar(d12[:, 0:CT], dcol, 0.0, None, op0=ALU.min))
                vop(lambda e: e.tensor_scalar(d12[:, CT:2 * CT], dcol, 0.0, -1.0,
                                              op0=ALU.max, op1=ALU.mult), inc="sv")
                sch.wait("scalar", "sv")
            else:
                vop(lambda e: e.nop(), inc="sv")
        sch.drain_inc = True
        sch.barrier(("vector", "scalar"))
        if c == 0:
            tap("pp0", ppF)
            tap("lg0", lgF)
        if c == 1:
            tap("pp1", ppF)
            tap("lg1", lgF)
        vop(lambda e: e.tensor_copy(ppcar, pp[:, :, TC]))
        vop(lambda e: e.tensor_copy(lgcar, lg[:, :, TC]))
        vop(lambda e: e.tensor_tensor(fl(s0, TC + 1), fl(pp, TC + 1), fl(lg, TC + 1), ALU.add))
        vop(lambda e: e.tensor_tensor(fl(s1, TC), fl(s0, TC), fl(kap, TC), ALU.subtract), inc="sv")
        sch.wait("scalar", "sv")
        act(fl(s2, TC), fl(s1, TC), AF.Abs)
        act(fl(lg, TC), fl(s2, TC), AF.Exp, scale=-1.0)
        act(fl(s2, TC), fl(lg, TC), AF.Ln, bias=1.0, inc="ss")
        sch.wait("vector", "ss")
        vop(lambda e: e.tensor_scalar(fl(lg, TC), fl(s1, TC), 0.0, None, op0=ALU.max))
        vop(lambda e: e.tensor_tensor(fl(lg, TC), fl(lg, TC), fl(s2, TC), ALU.add))
        vop(lambda e: e.tensor_tensor(fl(s2, TC), fl(s1, TC), fl(lg, TC), ALU.subtract), inc="sv")
        sch.wait("scalar", "sv")
        act(fl(s1, TC), fl(s2, TC), AF.Exp)
        act(fl(s2, TC), fl(lg, TC), AF.Exp, scale=-1.0, inc="ss")
        sch.wait("vector", "ss")
        vop(lambda e: e.tensor_tensor(fl(s2, TC), fl(s2, TC), fl(var, TC), ALU.mult), inc="sv")
        for ct in range(CT):
            vop(lambda e, ct=ct: e.tensor_copy(pp[:, ct, 0:1], carryA[:, ct:ct + 1]))
            vop(lambda e, ct=ct: e.tensor_tensor_scan(
                pp[:, ct, 1:TC + 1], s1[:, ct, 0:TC], s2[:, ct, 0:TC],
                carryA[:, ct:ct + 1], ALU.mult, ALU.add))
            vop(lambda e: e.drain())
            vop(lambda e, ct=ct: e.tensor_copy(carryA[:, ct:ct + 1], pp[:, ct, TC:TC + 1]), inc="sv")
        vop(lambda e: e.tensor_tensor(fl(s0, TC), fl(s0, TC), fl(kap, TC), ALU.subtract))
        for ct in range(CT):
            vop(lambda e, ct=ct: e.tensor_scalar(s0[:, ct, 0:TC], s0[:, ct, 0:TC],
                                                 sb_u[:, ct:ct + 1], None, op0=ALU.subtract))
        vop(lambda e: e.tensor_scalar(fl(s2, TC), fl(s0, TC), 0.0, None, op0=ALU.max))
        vop(lambda e: e.tensor_tensor(fl(s1, TC), fl(s2, TC), fl(s0, TC), ALU.subtract), inc="sv")
        sch.wait("scalar", "sv")
        act(fl(lg, TC), fl(s1, TC), AF.Exp, scale=-1.0)
        act(fl(s1, TC), fl(s2, TC), AF.Exp, scale=-1.0, inc="ss")
        sch.wait("vector", "ss")
        vop(lambda e: e.tensor_tensor(fl(s2, TC), fl(pp, TC), fl(lg, TC), ALU.mult))
        vop(lambda e: e.tensor_tensor(fl(lg, TC), fl(lg, TC), fl(s1, TC), ALU.add))
        vop(lambda e: e.tensor_tensor(fl(s1, TC), fl(s1, TC), fl(var, TC), ALU.mult))
        vop(lambda e: e.tensor_tensor(fl(s2, TC), fl(s2, TC), fl(s1, TC), ALU.add), inc="sv")
        sch.wait("scalar", "sv")
        act(fl(s1, TC), fl(lg, TC), AF.Ln)
        act(fl(lg, TC), fl(s1, TC), AF.Exp, scale=-1.0, inc="ss")
        sch.wait("vector", "ss")
        vop(lambda e: e.tensor_tensor(fl(s2, TC), fl(s2, TC), fl(lg, TC), ALU.mult), inc="sv")
        if c == 0:
            tap("y0", s2F)
            tap("A0", ppF)
        if c == 1:
            tap("y1", s2F)
            tap("A1", ppF)
        vop(lambda e, c=c: e.tensor_scalar(fl(s1, TC), fl(s2, TC), sb_selv[:, c:c + 1], None, op0=ALU.mult))
        vop(lambda e, c=c: e.tensor_scalar(dcol, s2[:, :, TC - 1], sb_selb[:, c:c + 1], None,
                                           op0=ALU.mult))
        vop(lambda e: e.drain(), inc="sv")
        sch.wait("gpsimd", "sv")
        off = (c % 4) * TC
        acc = None if c < 4 else ALU.add
        for ct in range(CT):
            dma("gp", yscr[ct * P:(ct + 1) * P, off:off + TC], s1[:, ct, 0:TC], q="qG", accum=acc)
        accb = None if c == 0 else ALU.add
        dma("gp", ybscr.rearrange("(t p) n -> p t n", p=P),
            dcol.rearrange("p (t n) -> p t n", n=1), q="qG", accum=accb, ncd=True)
        sch.wait("vector", "qG")
        sch.barrier()

    kkFs = [kapF, varF, ppF, lgF]

    def kk_view(ft, n=OB):
        return kkFs[ft // 8][:, (ft % 8) * OB:(ft % 8) * OB + n]

    for b in range(NB):
        base = b * OB
        load_fm(xnscr, base + 1, OB + 1, s0)
        load_fm(sigrscr, base, OB + 1, s1)
        if b == 0:
            for ct in range(CT):
                dma("sy", s2[:, ct, 0:1], ybscr[ct * P:(ct + 1) * P, 0:1], ncd=True)
                dma("sy", s2[:, ct, 1:OB + 1], yscr[ct * P:(ct + 1) * P, 0:OB])
        else:
            load_fm(yscr, base - 1, OB + 1, s2)
        sch.wait("vector", "qL")
        vop(lambda e: e.tensor_tensor(fl(s1, OB + 1), fl(s1, OB + 1), fl(s2, OB + 1), ALU.mult))
        vop(lambda e: e.tensor_tensor(fl(s0, OB + 1), fl(s0, OB + 1), fl(s1, OB + 1), ALU.add), inc="sv")
        tap("x1_b%d" % b, s0F)
        if b == 0:
            tap("x1", s0F)
        sch.wait("scalar", "sv")
        act(fl(s1, OB + 1), fl(s0, OB + 1), AF.Square, inc="ss")
        sch.wait("tensor", "ss")
        for src3, sr0 in ((s0, 0), (s1, OB + 1)):
            for h0, hn in ((0, 512), (512, 1)):
                for ct in range(CT):
                    sch.op("te", lambda e, src3=src3, ct=ct, h0=h0, hn=hn:
                           e.matmul(psR[:, 0:hn], sb_onescol, src3[:, ct, h0:h0 + hn],
                                    start=(ct == 0), stop=(ct == CT - 1)),
                           inc=("st" if ct == CT - 1 else None))
                sch.wait("vector", "st")
                vop(lambda e, sr0=sr0, h0=h0, hn=hn:
                    e.tensor_copy(statrow[:, sr0 + h0:sr0 + h0 + hn], psR[:, 0:hn]), inc="sv")
                sch.wait("tensor", "sv")
        srow = lambda o, n=OB + 1: statrow[:, o:o + n]
        vop(lambda e: e.drain())
        vop(lambda e: e.tensor_scalar(srow(0), srow(0), 1.0 / C, None, op0=ALU.mult))
        vop(lambda e: e.tensor_scalar(srow(OB + 1), srow(OB + 1), 1.0 / C, None, op0=ALU.mult))
        vop(lambda e: e.tensor_tensor(temp5[0:1, 0:OB + 1], srow(0), srow(0), ALU.mult))
        vop(lambda e: e.tensor_tensor(srow(OB + 1), srow(OB + 1), temp5[0:1, 0:OB + 1], ALU.subtract), inc="sv")
        sch.wait("scalar", "sv")
        vop(lambda e: e.tensor_scalar(tmp6[0:1, 0:OB + 1], srow(OB + 1), EPS, None, op0=ALU.add), inc="sv")
        sch.wait("scalar", "sv")
        act(temp5[0:1, 0:OB + 1], tmp6[0:1, 0:OB + 1], AF.Ln)
        act(srow(OB + 1), temp5[0:1, 0:OB + 1], AF.Exp, scale=-0.5, inc="ss")
        sch.wait("tensor", "ss")
        for sr0, dst in ((0, mb), (OB + 1, rb)):
            for h0, hn in ((0, 512), (512, 1)):
                sch.op("te", lambda e, sr0=sr0, h0=h0, hn=hn:
                       e.matmul(psB[:, 0:hn], sb_ones1, statrow[:, sr0 + h0:sr0 + h0 + hn],
                                start=True, stop=True), inc="st")
                sch.wait("vector", "st")
                vop(lambda e, dst=dst, h0=h0, hn=hn: e.tensor_copy(dst[:, h0:h0 + hn], psB[:, 0:hn]), inc="sv")
                sch.wait("tensor", "sv")
        vop(lambda e: e.drain())
        for ct in range(CT):
            nn = OB + 1
            vop(lambda e, ct=ct, nn=nn: e.tensor_tensor(s1[:, ct, 0:nn], s0[:, ct, 0:nn], mb[:, 0:nn], ALU.subtract))
            vop(lambda e, ct=ct, nn=nn: e.tensor_tensor(s1[:, ct, 0:nn], s1[:, ct, 0:nn], rb[:, 0:nn], ALU.mult))
            vop(lambda e, ct=ct, nn=nn: e.tensor_scalar(s1[:, ct, 0:nn], s1[:, ct, 0:nn],
                                                        sb_g2[:, ct:ct + 1], sb_b2[:, ct:ct + 1],
                                                        op0=ALU.mult, op1=ALU.add))
        vop(lambda e: e.nop(), inc="sv")
        if b == 0:
            vop(lambda e: e.tensor_scalar(t1c, s1[:, :, 0], bsel[:, 0:1], None, op0=ALU.mult))
            vop(lambda e: e.tensor_scalar(qc, sb_tsh, nbsel[:, 0:1], None, op0=ALU.mult))
            vop(lambda e: e.drain())
            vop(lambda e: e.tensor_tensor(s1[:, :, 0], t1c, qc, ALU.add), inc="sv")
        if b == 0:
            tap("h2", s1F)
        sch.barrier(("vector", "tensor", "scalar", "sync"))

        def evict_rr(mt, ps):
            act(temp5[:, 0:OB], ps[:, 0:OB], AF.Exp, scale=-1.0, inc="ss")
            sch.wait("vector", "ss")
            vop(lambda e: e.tensor_scalar(temp5[:, 0:OB], temp5[:, 0:OB], 1.0, None, op0=ALU.add), inc="sv")
            sch.wait("scalar", "sv")
            act(tmp6[:, 0:OB], temp5[:, 0:OB], AF.Ln, inc="ss")
            sch.wait("scalar", "ss")
            act(s2[:, mt, 0:OB], tmp6[:, 0:OB], AF.Exp, scale=-1.0, inc="ss")
            sch.wait("vector", "ss")
            vop(lambda e: e.nop(), inc="sv")

        emit_linear(evict_rr, Wcr, s1, OB, CT)
        if b == 0:
            tap("rr", s2F)
        sch.barrier(("vector", "tensor", "scalar", "sync"))

        def evict_kk(ft, ps):
            act(temp5[:, 0:OB], ps[:, 0:OB], AF.Relu, inc="ss")
            sch.wait("vector", "ss")
            vop(lambda e, ft=ft: e.tensor_tensor(kk_view(ft), temp5[:, 0:OB], temp5[:, 0:OB], ALU.mult), inc="sv")

        emit_linear(evict_kk, Wck, s1, OB, FT)
        if b == 0:
            tap("kk0", kapF)
        sch.barrier(("vector", "tensor", "scalar", "sync"))

        for mt in range(CT):
            sch.wait("sync", "sv")
            sch.wait("sync", "st")
            dma("sy", wfbuf[:, 0:16 * P].rearrange("p (t n) -> p t n", t=16),
                Wcv[:, 0:16, mt * P:(mt + 1) * P])
            dma("sy", wfbuf2[:, 0:16 * P].rearrange("p (t n) -> p t n", t=16),
                Wcv[:, 16:32, mt * P:(mt + 1) * P])
            sch.wait("tensor", "qL")
            sch.wait("tensor", "sv")
            for kt in range(FT):
                lhsT = (wfbuf[:, kt * P:(kt + 1) * P] if kt < 16
                        else wfbuf2[:, (kt - 16) * P:(kt - 15) * P])
                sch.op("te", lambda e, lhsT=lhsT, kt=kt: e.matmul(
                    psK[:, 0:OB], lhsT, kk_view(kt), start=(kt == 0), stop=(kt == FT - 1)),
                    inc=("st" if kt == FT - 1 else None))
            sch.wait("vector", "st")
            vop(lambda e, mt=mt: e.tensor_tensor(temp5[:, 0:OB], s2[:, mt, 0:OB], psK[:, 0:OB], ALU.mult))
            vop(lambda e, mt=mt: e.tensor_tensor(s2[:, mt, 0:OB], s0[:, mt, 1:OB + 1], temp5[:, 0:OB], ALU.add), inc="sv")
        fbar(s2F)
        tap("sattn_b%d" % b, s2F)
        gptap("gp_sattn_b%d" % b, s2F)
        if b == 0:
            tap("sattn", s2F)
        sch.barrier(("vector", "tensor", "scalar", "sync"))

        load_fm(xnscr, base + 2, OB, s1)
        for ex_ in range(NE):
            dma("sy", cwland[0:1, ex_ * OB:(ex_ + 1) * OB], cwscr[ex_:ex_ + 1, base:base + OB])
        sch.wait("vector", "qL")
        vop(lambda e: e.memset(fl(s0, OB), 0.0), inc="sv")
        sch.wait("tensor", "sv")
        for ex in range(NE):
            sch.op("te", lambda e, ex=ex: e.matmul(psB[:, 0:OB], sb_ones1,
                                                   cwland[0:1, ex * OB:(ex + 1) * OB], start=True, stop=True), inc="st")
            sch.wait("vector", "st")
            vop(lambda e: e.tensor_copy(mb[:, 0:OB], psB[:, 0:OB]), inc="sv")
            sch.wait("tensor", "sv")
            for fc in range(NFC):
                f0 = fc * FCH
                sch.wait("sync", "st")
                dma("sy", wfbuf.rearrange("p (t n) -> p t n", t=CT),
                    Wg[ex, :, :, f0:f0 + FCH])
                dma("sy", wfbuf2.rearrange("p (t n) -> p t n", t=CT),
                    Wu[ex, :, :, f0:f0 + FCH])
                sch.wait("tensor", "qL")
                for sub in range(FCH // P):
                    ft = fc * (FCH // P) + sub
                    sch.wait("tensor", "sv")
                    for kt in range(CT):
                        sch.op("te", lambda e, kt=kt, sub=sub: e.matmul(
                            psG[:, 0:OB], wfbuf[:, kt * FCH + sub * P: kt * FCH + (sub + 1) * P],
                            s1[:, kt, 0:OB], start=(kt == 0), stop=(kt == CT - 1)),
                            inc=("st" if kt == CT - 1 else None))
                    for kt in range(CT):
                        sch.op("te", lambda e, kt=kt, sub=sub: e.matmul(
                            psU[:, 0:OB], wfbuf2[:, kt * FCH + sub * P: kt * FCH + (sub + 1) * P],
                            s1[:, kt, 0:OB], start=(kt == 0), stop=(kt == CT - 1)),
                            inc=("st" if kt == CT - 1 else None))
                    sch.wait("vector", "st")
                    sch.wait("scalar", "st")
                    act(temp5[:, 0:OB], psG[:, 0:OB], AF.Exp, scale=-1.0, inc="ss")
                    sch.wait("vector", "ss")
                    vop(lambda e: e.tensor_scalar(temp5[:, 0:OB], temp5[:, 0:OB], 1.0, None, op0=ALU.add), inc="sv")
                    sch.wait("scalar", "sv")
                    act(rb[:, 0:OB], temp5[:, 0:OB], AF.Ln, inc="ss")
                    sch.wait("scalar", "ss")
                    act(tmp6[:, 0:OB], rb[:, 0:OB], AF.Exp, scale=-1.0, inc="ss")
                    sch.wait("vector", "ss")
                    vop(lambda e: e.tensor_tensor(temp5[:, 0:OB], psU[:, 0:OB], mb[:, 0:OB], ALU.mult))
                    vop(lambda e, ft=ft: e.tensor_tensor(kk_view(ft), psG[:, 0:OB], temp5[:, 0:OB], ALU.mult))
                    vop(lambda e, ft=ft: e.tensor_tensor(kk_view(ft), kk_view(ft), tmp6[:, 0:OB], ALU.mult), inc="sv")
            for mt in range(CT):
                sch.wait("sync", "sv")
                sch.wait("sync", "st")
                dma("sy", wfbuf[:, 0:16 * P].rearrange("p (t n) -> p t n", t=16),
                    Wd[ex, :, 0:16, mt * P:(mt + 1) * P])
                dma("sy", wfbuf2[:, 0:16 * P].rearrange("p (t n) -> p t n", t=16),
                    Wd[ex, :, 16:32, mt * P:(mt + 1) * P])
                sch.wait("tensor", "qL")
                sch.wait("tensor", "sv")
                if b == 1 and ex == 7 and mt == 7:
                    tap("wdump", wfbuf)
                    tap("wdump2", wfbuf2)
                    tap("harena", lgF)
                for kt in range(FT):
                    lhsT = (wfbuf[:, kt * P:(kt + 1) * P] if kt < 16
                            else wfbuf2[:, (kt - 16) * P:(kt - 15) * P])
                    sch.op("te", lambda e, lhsT=lhsT, kt=kt: e.matmul(
                        psD[:, 0:OB], lhsT, kk_view(kt), start=(kt == 0), stop=(kt == FT - 1)),
                        inc=("st" if kt == FT - 1 else None))
                sch.wait("vector", "st")
                vop(lambda e, mt=mt: e.tensor_tensor(s0[:, mt, 0:OB], s0[:, mt, 0:OB], psD[:, 0:OB], ALU.add), inc="sv")
            sch.barrier(("vector", "tensor", "scalar", "sync"))

        fbar(s0F)
        tap("ffn_b%d" % b, s0F)
        gptap("gp_ffn_b%d" % b, s0F)
        gptap("gp_s2post_b%d" % b, s2F)
        if b == 0:
            tap("ffn", s0F)
        W513 = CT * (TC + 1)
        act(s1F[:, 0:W513], s0F[:, 0:W513], AF.Exp, scale=-1.0, inc="ss")
        sch.wait("vector", "ss")
        vop(lambda e: e.tensor_scalar(s1F[:, 0:W513], s1F[:, 0:W513], 1.0, None, op0=ALU.add))
        vop(lambda e: e.drain())
        vop(lambda e: e.reciprocal(lgF[:, 0:W513], s1F[:, 0:W513]))
        vop(lambda e: e.drain())
        if b == 0:
            tap("gsil", s0F)
            tap("sattn2", s2F)
        vop(lambda e: e.tensor_tensor(s0F[:, 0:W513], s0F[:, 0:W513], lgF[:, 0:W513], ALU.mult))
        vop(lambda e: e.tensor_tensor(s2F[:, 0:W513], s2F[:, 0:W513], s0F[:, 0:W513], ALU.mult), inc="sv")
        tap("outfm_b%d" % b, s2F)
        gptap("gp_outfm_b%d" % b, s2F)
        if b == 0:
            tap("outfm", s2F)
        for q4 in range(4):
            sch.wait("sync", "sv")
            dma("sy", xt, xob[2 + base + q4 * P: 2 + base + (q4 + 1) * P, :])
            for ct in range(CT):
                sch.wait("tensor", "sv")
                sch.op("te", lambda e, ct=ct, q4=q4: e.transpose(
                    psT, s2[:, ct, q4 * P:(q4 + 1) * P], ident), inc="st")
                sch.wait("vector", "st")
                vop(lambda e, ct=ct: e.tensor_copy(h_t[:, ct * P:(ct + 1) * P], psT), inc="sv")
            sch.wait("vector", "qL")
            vop(lambda e: e.drain())
            vop(lambda e: e.tensor_tensor(h_t, h_t, xt, ALU.add), inc="sv")
            sch.wait("sync", "sv")
            dma("sy", out_own[base + q4 * P: base + (q4 + 1) * P, :], h_t, q="qO")
            sch.wait("vector", "qO")
        sch.barrier()

    for nm, scr, cols in (("yscr", yscr, S), ("ybscr", ybscr, 1), ("xnscr", xnscr, 2176),
                          ("hscr", hscr, 2176), ("sigrscr", sigrscr, S + 1)):
        if DBG.get(nm):
            t = nc.dram_tensor("dbg_" + nm, [C, cols], F32, kind="ExternalOutput").ap()
            dma("sy", t, scr, q="qO")
    if DBG.get("cwscr"):
        t = nc.dram_tensor("dbg_cwscr", [NE, S], F32, kind="ExternalOutput").ap()
        dma("sy", t, cwscr, q="qO")
    sch.wait("sync", "qO")
    sch.wait("sync", "qG")

    with nc.Block() as blk:
        @blk.sync
        def _(eng):
            for f in sch.ops["sync"]:
                f(eng)

        @blk.gpsimd
        def _(eng):
            for f in sch.ops["gpsimd"]:
                f(eng)

        @blk.tensor
        def _(eng):
            for f in sch.ops["tensor"]:
                f(eng)

        @blk.vector
        def _(eng):
            for f in sch.ops["vector"]:
                f(eng)

        @blk.scalar
        def _(eng):
            for f in sch.ops["scalar"]:
                f(eng)

    return nc


def _tile_w(w):
    K, M = w.shape
    return np.ascontiguousarray(w.reshape(K // P, P, M).transpose(1, 0, 2))


def kernel(**inputs):
    x = np.asarray(inputs["L0_attn_x"], np.float32)[0]
    w = np.asarray(inputs["L0_attn_time_decay"], np.float32)
    u = np.asarray(inputs["L0_attn_time_first"], np.float32)
    tsh = np.asarray(inputs["L0_attn_time_shift"], np.float32)[0, 0]
    g1 = np.asarray(inputs["L0_attn_ln1_g"], np.float32)
    b1 = np.asarray(inputs["L0_attn_ln1_b"], np.float32)
    g2 = np.asarray(inputs["L0_attn_ln2_g"], np.float32)
    b2 = np.asarray(inputs["L0_attn_ln2_b"], np.float32)

    com = {
        "x": np.ascontiguousarray(x),
        "Wk": _tile_w(np.asarray(inputs["L0_attn_Wk"], np.float32)),
        "Wv": _tile_w(np.asarray(inputs["L0_attn_Wv"], np.float32)),
        "Wr": _tile_w(np.asarray(inputs["L0_attn_Wr"], np.float32)),
        "Wck": _tile_w(np.asarray(inputs["L0_attn_Wck"], np.float32)),
        "Wcv": _tile_w(np.asarray(inputs["L0_attn_Wcv"], np.float32)),
        "Wcr": _tile_w(np.asarray(inputs["L0_attn_Wcr"], np.float32)),
        "Wg": np.stack([_tile_w(np.asarray(inputs["L0_ffn_w_gate"], np.float32)[e]) for e in range(NE)]),
        "Wu": np.stack([_tile_w(np.asarray(inputs["L0_ffn_w_up"], np.float32)[e]) for e in range(NE)]),
        "Wd": np.stack([_tile_w(np.asarray(inputs["L0_ffn_w_down"], np.float32)[e]) for e in range(NE)]),
        "Wrg": _tile_w(np.concatenate([np.asarray(inputs["L0_ffn_w_router"], np.float32),
                                       np.asarray(inputs["L0_ffn_w_gating"], np.float32)], axis=1)),
        "wrow8": np.ascontiguousarray(w.reshape(1, CT * P)),
        "ucol": np.ascontiguousarray(u.reshape(CT, P).T),
        "g2col": np.ascontiguousarray(g2.reshape(CT, P).T),
        "b2col": np.ascontiguousarray(b2.reshape(CT, P).T),
        "tshcol": np.ascontiguousarray(tsh.reshape(CT, P).T),
        "g1row": g1.reshape(1, C),
        "b1row": b1.reshape(1, C),
        "niota": (-np.arange(T, dtype=np.float32)).reshape(1, T),
        "identin": np.eye(P, dtype=np.float32),
        "ones1": np.ones((1, P), np.float32),
        "onescol": np.ones((P, 1), np.float32),
    }
    in_maps = []
    for j in range(NCOR):
        s0_, e0_ = j * S, (j + 1) * S
        xo = np.zeros((2176, C), np.float32)
        if j == 0:
            xo[2:2050] = x[0:e0_]
        else:
            xo[0:2050] = x[s0_ - 2:e0_]
        sv = np.zeros((P, NCH), np.float32)
        sv[:, 4 * j:4 * j + 4] = 1.0
        sb = np.zeros((P, NCH), np.float32)
        if j > 0:
            sb[:, 4 * j - 1] = 1.0
        bs = np.full((P, 1), 0.0 if j == 0 else 1.0, np.float32)
        m = dict(com)
        m.update({"xob": xo, "selv": sv, "selb": sb, "bsel": bs,
                  "nbsel": (1.0 - bs).astype(np.float32)})
        in_maps.append(m)

    nc = build()
    res = run_bass_kernel_spmd(nc, in_maps, core_ids=list(range(NCOR)))
    out = np.concatenate([np.asarray(res.results[j]["out_own"]) for j in range(NCOR)], axis=0)
    return out.reshape(1, T, C).astype(np.float32)
